# version 02
# baseline (speedup 1.0000x reference)
import contextlib
import numpy as np
import ml_dtypes
import concourse.bass as bass
import concourse.mybir as mybir
from concourse.bass_utils import run_bass_kernel_spmd

F32 = mybir.dt.float32
BF16 = mybir.dt.bfloat16
AF = mybir.ActivationFunctionType
ALU = mybir.AluOpType
AX = mybir.AxisListType

S, D = 4096, 1024
NCH, CH, A = 4, 1024, 8
NH, HD = 16, 64
NLH, DIN = 4, 256
DFF = 4096
EPS = 1e-5
RC = 12582912.0
BF = ml_dtypes.bfloat16


class Sem:
    def __init__(self, stack, nc, name):
        self.h = stack.enter_context(nc.semaphore(name))
        self.n = 0


def build(dbg=(), nphase=5, p1n=NCH, p1h=NLH, p1lact=True, p1stage=9, p2og=8, p2stage=9, p2w=32):
    nc = bass.Bass("TRN2", target_bir_lowering=False)

    d_x_tm = nc.dram_tensor("x_tm", [S, D], F32, kind="ExternalInput").ap()
    d_x_fm = nc.dram_tensor("x_fm", [D, S], F32, kind="ExternalInput").ap()
    d_wqkvT = nc.dram_tensor("wqkvT", [D, 3 * D], F32, kind="ExternalInput").ap()
    d_woTh = nc.dram_tensor("woTh", [64, 16 * D], F32, kind="ExternalInput").ap()
    d_bcols = nc.dram_tensor("bcols", [128, 24], F32, kind="ExternalInput").ap()
    d_brow = nc.dram_tensor("brow", [1, 3 * D], F32, kind="ExternalInput").ap()
    d_wob = nc.dram_tensor("wob", [128, 8], F32, kind="ExternalInput").ap()
    d_lw0T = nc.dram_tensor("lw0T", [128, 2048], F32, kind="ExternalInput").ap()
    d_lw2T = nc.dram_tensor("lw2T", [128, 2048], F32, kind="ExternalInput").ap()
    d_lw1 = nc.dram_tensor("lw1", [128, 2048], F32, kind="ExternalInput").ap()
    d_lw1T = nc.dram_tensor("lw1T", [128, 2048], F32, kind="ExternalInput").ap()
    d_lnw = nc.dram_tensor("lnw", [1, D], F32, kind="ExternalInput").ap()
    d_lnb = nc.dram_tensor("lnb", [1, D], F32, kind="ExternalInput").ap()
    d_fw1T = nc.dram_tensor("fw1T", [D, DFF], F32, kind="ExternalInput").ap()
    d_fw2T = nc.dram_tensor("fw2T", [DFF, D], F32, kind="ExternalInput").ap()
    d_b1c = nc.dram_tensor("b1c", [128, 32], F32, kind="ExternalInput").ap()
    d_b2c = nc.dram_tensor("b2c", [128, 8], F32, kind="ExternalInput").ap()
    d_out = nc.dram_tensor("out_fm", [D, S], F32, kind="ExternalOutput").ap()
    d_scrl = nc.dram_tensor("scr_lact", [D, S], F32).ap()
    d_xn2fm = nc.dram_tensor("scr_xn2fm", [D, S], BF16).ap()
    d_scra = nc.dram_tensor("scr_attn", [D, S], F32).ap()
    d_scrao = nc.dram_tensor("scr_ao", [D, S], BF16).ap()
    d_strows = nc.dram_tensor("scr_strows", [2, S], F32).ap()
    d_gg = nc.dram_tensor("scr_gg", [128, 2], F32).ap()
    d_m0 = nc.dram_tensor("scr_m0", [32, 128], BF16).ap()
    d_m0b = nc.dram_tensor("scr_m0b", [32, 128], BF16).ap()
    d_dbgm = nc.dram_tensor("scr_dbgm", [16, S], BF16).ap()
    d_dbgqk = nc.dram_tensor("scr_dbgqk", [128, S], BF16).ap()
    d_dbgq2 = nc.dram_tensor("scr_dbgq2", [128, S], BF16).ap()
    d_dbgw = nc.dram_tensor("scr_dbgw", [128, 3072], BF16).ap()
    d_identf = nc.inline_tensor(np.eye(128, dtype=np.float32), name="ident128f").ap()
    d_ident = nc.inline_tensor(np.eye(128, dtype=BF), name="ident128").ap()

    dbg_taps = {}

    def tap(name, src, shape, dtype=F32):
        if name in dbg:
            ap = nc.dram_tensor("dbg_" + name, shape, dtype, kind="ExternalOutput").ap()
            dbg_taps[name] = (ap, src)

    st = contextlib.ExitStack()
    with st:
        e = st.enter_context

        bcols = e(nc.sbuf_tensor("bcols_sb", [128, 24], F32))
        bq2 = e(nc.sbuf_tensor("bq2", [128, 8], F32))
        wob = e(nc.sbuf_tensor("wob_sb", [128, 8], F32))
        b1c = e(nc.sbuf_tensor("b1c_sb", [128, 32], F32))
        b2c = e(nc.sbuf_tensor("b2c_sb", [128, 8], F32))
        ident = e(nc.sbuf_tensor("ident_sb", [128, 128], BF16))
        identf = e(nc.sbuf_tensor("identf_sb", [128, 128], F32))
        stats_a = e(nc.sbuf_tensor("stats_a", [128, 64], F32))
        r_bc = e(nc.sbuf_tensor("r_bc", [128, 2], F32))
        O_f = e(nc.sbuf_tensor("O_f", [128, 128], F32))
        O64 = e(nc.sbuf_tensor("O64", [128, 128], F32))
        pF = e(nc.psum_tensor("pF", [128, 4096], F32))

        sQ = Sem(st, nc, "sQ")
        sA = Sem(st, nc, "sA")
        sG = Sem(st, nc, "sG")
        sPE = Sem(st, nc, "sPE")
        sV = Sem(st, nc, "sV")
        sS = Sem(st, nc, "sS")
        sGP = Sem(st, nc, "sGP")

        block = e(nc.Block())

        def finish():
            @block.sync
            def _(sync):
                sync.wait_ge(sA.h, sA.n)
                sync.wait_ge(sG.h, sG.n)
                sync.wait_ge(sV.h, sV.n)
                sync.wait_ge(sPE.h, sPE.n)
                sync.wait_ge(sS.h, sS.n)
                sync.wait_ge(sGP.h, sGP.n)
                sync.wait_ge(sQ.h, sQ.n)
                k = 0
                for name, (ap, src) in dbg_taps.items():
                    sync.dma_start(out=ap, in_=src).then_inc(sQ.h, 16)
                    k += 1
                if nphase < 5:
                    sync.dma_start(out=d_out[0:128, 0:24], in_=bcols[:]).then_inc(sQ.h, 16)
                    k += 1
                sQ.n += k * 16
                sync.wait_ge(sQ.h, sQ.n)

        @block.sync
        def _(sync):
            sync.dma_start(out=ident[:], in_=d_ident[:]).then_inc(sQ.h, 16)
            sync.dma_start(out=identf[:], in_=d_identf[:]).then_inc(sQ.h, 16)
            sync.dma_start(out=bcols[:], in_=d_bcols[:]).then_inc(sQ.h, 16)
            sync.dma_start(out=wob[:], in_=d_wob[:]).then_inc(sQ.h, 16)
            sync.dma_start(out=b1c[:], in_=d_b1c[:]).then_inc(sQ.h, 16)
            sync.dma_start(out=b2c[:], in_=d_b2c[:]).then_inc(sQ.h, 16)
        sQ.n += 6 * 16
        tok_su_dma = sQ.n

        @block.vector
        def _(v):
            v.wait_ge(sQ.h, tok_su_dma)
            v.memset(O_f[:], 0.0)
            v.memset(O64[:], 0.0)
            v.drain()
            v.memset(O_f[0:1, :], 1.0)
            v.memset(O64[64:65, :], 1.0)
            v.tensor_scalar(out=bq2[:], in0=bcols[:, 0:8], scalar1=0.125,
                            scalar2=None, op0=ALU.mult).then_inc(sV.h, 1)
        sV.n += 1

        s1 = contextlib.ExitStack()
        with s1:
            e1 = s1.enter_context
            lnw_bc = e1(nc.sbuf_tensor("lnw_bc", [128, D], F32))
            lnb_bc = e1(nc.sbuf_tensor("lnb_bc", [128, D], F32))
            lw0T = e1(nc.sbuf_tensor("lw0T_sb", [128, 2048], F32))
            lw2T = e1(nc.sbuf_tensor("lw2T_sb", [128, 2048], F32))
            lw1 = e1(nc.sbuf_tensor("lw1_sb", [128, 2048], F32))
            lw1T = e1(nc.sbuf_tensor("lw1T_sb", [128, 2048], F32))
            x_ch = e1(nc.sbuf_tensor("x_ch", [128, 8192], F32))
            cfm_f32 = e1(nc.sbuf_tensor("cfm_f32", [128, 8192], F32))
            cfm_bf = e1(nc.sbuf_tensor("cfm_bf", [128, 8192], BF16))
            bnb = e1(nc.sbuf_tensor("bnb", [128, 96], F32))
            mv1 = e1(nc.sbuf_tensor("mv1", [128, 16], F32))
            mv2 = e1(nc.sbuf_tensor("mv2", [128, 16], F32))
            veps = e1(nc.sbuf_tensor("veps", [128, 16], F32))
            srt = e1(nc.sbuf_tensor("srt", [128, 16], F32))
            nt0 = e1(nc.sbuf_tensor("nt0", [128, 16], F32))
            nt1 = e1(nc.sbuf_tensor("nt1", [128, 16], F32))
            rst1 = e1(nc.sbuf_tensor("rst1", [128, 8], F32))
            rst2 = e1(nc.sbuf_tensor("rst2", [128, 8], F32))
            sg_sb = e1(nc.sbuf_tensor("sg_sb", [128, 2048], F32))
            h_sb = e1(nc.sbuf_tensor("h_sb", [128, 2048], F32))
            t_sb = e1(nc.sbuf_tensor("t_sb", [128, 2048], F32))
            u_sb = e1(nc.sbuf_tensor("u_sb", [128, 2048], F32))
            hid_bf = e1(nc.sbuf_tensor("hid_bf", [128, 2048], F32))
            pgu_bf = e1(nc.sbuf_tensor("pgu_bf", [128, 2048], F32))
            ph_bf = e1(nc.sbuf_tensor("ph_bf", [128, 2048], F32))
            wn1 = e1(nc.sbuf_tensor("wn1", [128, 512], F32))
            wn0 = e1(nc.sbuf_tensor("wn0", [128, 512], F32))
            wn2 = e1(nc.sbuf_tensor("wn2", [128, 512], F32))
            oq_sb = e1(nc.sbuf_tensor("oq_sb", [128, 1024], F32))

            pA = pF[:, 0:2048]
            pB = pF[:, 2048:4096]

            @block.sync
            def _(sync):
                sync.dma_start(out=lnw_bc[:],
                               in_=d_lnw[:].partition_broadcast(128)).then_inc(sQ.h, 16)
                sync.dma_start(out=lnb_bc[:],
                               in_=d_lnb[:].partition_broadcast(128)).then_inc(sQ.h, 16)
            sQ.n += 32
            tok_lnr = sQ.n

            @block.gpsimd
            def _(gp):
                gp.dma_start(out=lw0T[:], in_=d_lw0T[:]).then_inc(sG.h, 16)
                gp.dma_start(out=lw2T[:], in_=d_lw2T[:]).then_inc(sG.h, 16)
                gp.dma_start(out=lw1[:], in_=d_lw1[:]).then_inc(sG.h, 16)
                gp.dma_start(out=lw1T[:], in_=d_lw1T[:]).then_inc(sG.h, 16)
            sG.n += 64
            tok_lwload = sG.n

            tok_lnbc_q = tok_lnr
            tok_xch_free = 0
            tok_xn2_pe = 0
            tok_oqprev = 0
            tok_scrl = 0

            def bn8(v, buf, mv):
                for t in range(A):
                    v.bn_stats(out=bnb[:, t * 12:t * 12 + 6],
                               in_=buf[:, t * 1024:t * 1024 + 512])
                    v.bn_stats(out=bnb[:, t * 12 + 6:t * 12 + 12],
                               in_=buf[:, t * 1024 + 512:t * 1024 + 1024])
                v.drain()
                for t in range(A):
                    v.bn_aggr(out=mv[:, t * 2:t * 2 + 2], in_=bnb[:, t * 12:t * 12 + 12])
                v.drain()
                v.tensor_copy(out=veps[:, 0:8], in_=mv[:, 1:16:2])
                v.drain()
                v.tensor_scalar(out=veps[:, 0:8], in0=veps[:, 0:8], scalar1=EPS,
                                scalar2=None, op0=ALU.add)
                v.drain().then_inc(sV.h, 1)

            def newton(v, lo, rst):
                v.reciprocal(out=nt0[:, lo:lo + 8], in_=srt[:, lo:lo + 8])
                v.drain()
                v.tensor_tensor(out=nt1[:, lo:lo + 8], in0=nt0[:, lo:lo + 8],
                                in1=nt0[:, lo:lo + 8], op=ALU.mult)
                v.drain()
                v.tensor_tensor(out=nt1[:, lo:lo + 8], in0=nt1[:, lo:lo + 8],
                                in1=veps[:, 0:8], op=ALU.mult)
                v.drain()
                v.tensor_scalar(out=nt1[:, lo:lo + 8], in0=nt1[:, lo:lo + 8],
                                scalar1=-0.5, scalar2=1.5, op0=ALU.mult, op1=ALU.add)
                v.drain()
                v.tensor_tensor(out=rst[:], in0=nt0[:, lo:lo + 8],
                                in1=nt1[:, lo:lo + 8], op=ALU.mult)
                v.drain()

            tok_ctm_pe = 0
            tok_fmst_prev = 0
            for n in range(p1n):
                @block.gpsimd
                def _(gp, n=n, w=tok_xch_free, tpe=tok_ctm_pe):
                    if w:
                        gp.wait_ge(sV.h, w)
                    if tpe:
                        gp.wait_ge(sPE.h, tpe)
                    gp.dma_start(
                        out=x_ch[:].rearrange("p (a d) -> p a d", a=8),
                        in_=d_x_tm[n * CH:(n + 1) * CH, :].rearrange(
                            "(a p) d -> p a d", p=128),
                    ).then_inc(sG.h, 16)
                sG.n += 16
                tok_x = sG.n

                @block.vector
                def _(v, tok_x=tok_x):
                    v.wait_ge(sG.h, tok_x)
                    bn8(v, x_ch, mv1)
                sV.n += 1
                tok_bn1 = sV.n

                @block.scalar
                def _(sc, t=tok_bn1):
                    sc.wait_ge(sV.h, t)
                    sc.activation(out=srt[:, 0:8], in_=veps[:, 0:8],
                                  func=AF.Sqrt).then_inc(sS.h, 1)
                sS.n += 1
                tok_sq1 = sS.n

                @block.vector
                def _(v, t=tok_sq1, w=tok_xn2_pe):
                    v.wait_ge(sS.h, t)
                    if w:
                        v.wait_ge(sPE.h, w)
                    newton(v, 0, rst1)
                    for t2 in range(A):
                        v.tensor_scalar(out=x_ch[:, t2 * 1024:(t2 + 1) * 1024],
                                        in0=x_ch[:, t2 * 1024:(t2 + 1) * 1024],
                                        scalar1=mv1[:, 2 * t2:2 * t2 + 1],
                                        scalar2=rst1[:, t2:t2 + 1],
                                        op0=ALU.subtract, op1=ALU.mult)
                    v.drain()
                    bn8(v, x_ch, mv2)
                sV.n += 1
                tok_bn2 = sV.n

                @block.scalar
                def _(sc, t=tok_bn2):
                    sc.wait_ge(sV.h, t)
                    sc.activation(out=srt[:, 8:16], in_=veps[:, 0:8],
                                  func=AF.Sqrt).then_inc(sS.h, 1)
                sS.n += 1
                tok_sq2 = sS.n

                @block.vector
                def _(v, t=tok_sq2, n=n, tb=tok_lnbc_q):
                    v.wait_ge(sS.h, t)
                    v.wait_ge(sQ.h, tb)
                    newton(v, 8, rst2)
                    v.tensor_copy(out=stats_a[:, n * 16:n * 16 + 8], in_=mv1[:, 0:16:2])
                    v.tensor_copy(out=stats_a[:, n * 16 + 8:n * 16 + 16], in_=rst1[:])
                    for t2 in range(A):
                        v.tensor_scalar(out=x_ch[:, t2 * 1024:(t2 + 1) * 1024],
                                        in0=x_ch[:, t2 * 1024:(t2 + 1) * 1024],
                                        scalar1=mv2[:, 2 * t2:2 * t2 + 1],
                                        scalar2=rst2[:, t2:t2 + 1],
                                        op0=ALU.subtract, op1=ALU.mult)
                    v.drain()
                    for t2 in range(A):
                        v.tensor_tensor(out=x_ch[:, t2 * 1024:(t2 + 1) * 1024],
                                        in0=x_ch[:, t2 * 1024:(t2 + 1) * 1024],
                                        in1=lnw_bc[:], op=ALU.mult)
                    v.drain()
                    for t2 in range(A):
                        v.tensor_tensor(out=x_ch[:, t2 * 1024:(t2 + 1) * 1024],
                                        in0=x_ch[:, t2 * 1024:(t2 + 1) * 1024],
                                        in1=lnb_bc[:], op=ALU.add)
                    v.drain().then_inc(sV.h, 1)
                sV.n += 1
                tok_xn2 = sV.n
                tok_xch_free = tok_xn2

                tok_tre = 0
                for r in range(4):
                    @block.tensor
                    def _(te, n=n, r=r, t=tok_xn2, tprev=tok_oqprev,
                          te_ev=tok_tre):
                        if r == 0:
                            te.wait_ge(sV.h, max(t, tprev) if tprev else t)
                        else:
                            te.wait_ge(sS.h, te_ev)
                        for q in range(2):
                            dw = r * 2 + q
                            for a in range(A):
                                _mi = te.matmul(
                                    pA[:, q * 1024 + a * 128:q * 1024 + a * 128 + 128],
                                    x_ch[:, a * 1024 + dw * 128:a * 1024 + dw * 128 + 128],
                                    identf[:], start=True, stop=True)
                        _mi.then_inc(sPE.h, 1)
                    sPE.n += 1
                    tok_trm = sPE.n

                    @block.scalar
                    def _(sc, r=r, t=tok_trm):
                        sc.wait_ge(sPE.h, t)
                        sc.activation(out=cfm_f32[:, r * 2048:(r + 1) * 2048],
                                      in_=pA[:], func=AF.Identity).then_inc(sS.h, 1)
                    sS.n += 1
                    tok_tre = sS.n

                @block.vector
                def _(v, t=tok_tre, tq=tok_fmst_prev):
                    v.wait_ge(sS.h, t)
                    if tq:
                        v.wait_ge(sQ.h, tq)
                    v.tensor_copy(out=cfm_bf[:], in_=cfm_f32[:]
                                  ).then_inc(sV.h, 1)
                sV.n += 1
                tok_cbf = sV.n

                @block.sync
                def _(sync, n=n, t=tok_cbf):
                    sync.wait_ge(sV.h, t)
                    sync.dma_start(
                        out=d_xn2fm.rearrange("(fc p) t -> p fc t", p=128)[
                            :, :, n * CH:(n + 1) * CH],
                        in_=cfm_bf[:].rearrange("p (fc t) -> p fc t", fc=8),
                    ).then_inc(sQ.h, 16)
                sQ.n += 16
                tok_fmst_prev = sQ.n

                for h in range(NLH if p1lact else 0):
                    if h >= p1h:
                        continue
                    def cfm(j, a_, h=h):
                        fc = 2 * h + j
                        return cfm_f32[:, fc * 1024 + a_ * 128:
                                       fc * 1024 + a_ * 128 + 128]

                    def cfmh(j, t2, h=h):
                        fc = 2 * h + j
                        return cfm_f32[:, fc * 1024 + t2 * 512:
                                       fc * 1024 + t2 * 512 + 512]

                    def ctm(a_, j=None, h=h):
                        b = a_ * 1024 + h * 256
                        if j is None:
                            return x_ch[:, b:b + 256]
                        return x_ch[:, b + j * 128:b + j * 128 + 128]

                    @block.tensor
                    def _(te, h=h, ttre=tok_tre, tlw=tok_lwload,
                          tprev=tok_oqprev):
                        te.wait_ge(sS.h, ttre)
                        te.wait_ge(sG.h, tlw)
                        if tprev:
                            te.wait_ge(sV.h, tprev)
                        for a in range(A):
                            for j in range(2):
                                _mi = te.matmul(pA[:, a * 256:(a + 1) * 256], cfm(j, a),
                                          lw0T[:, (h * 2 + j) * 256:(h * 2 + j + 1) * 256],
                                          start=(a % 2 == 0 and j == 0), stop=(j == 1))
                        for a in range(A):
                            for j in range(2):
                                _mi = te.matmul(pB[:, a * 256:(a + 1) * 256], cfm(j, a),
                                          lw2T[:, (h * 2 + j) * 256:(h * 2 + j + 1) * 256],
                                          start=(a % 2 == 0 and j == 0), stop=(j == 1))
                        _mi.then_inc(sPE.h, 1)
                    sPE.n += 1
                    tok_gg_ = sPE.n

                    if p1stage < 1:
                        @block.vector
                        def _(v, t=tok_gg_):
                            v.wait_ge(sPE.h, t)
                            v.tensor_copy(out=sg_sb[:], in_=pA[:])
                            v.drain().then_inc(sV.h, 2)
                        sV.n += 2
                        continue
                    @block.scalar
                    def _(sc, t=tok_gg_):
                        sc.wait_ge(sPE.h, t)
                        sc.activation(out=sg_sb[:], in_=pA[:], func=AF.Sigmoid
                                      ).then_inc(sS.h, 1)
                    sS.n += 1
                    tok_sg = sS.n

                    @block.vector
                    def _(v, t=tok_sg):
                        v.wait_ge(sS.h, t)
                        v.tensor_tensor(out=h_sb[:], in0=pA[:], in1=sg_sb[:], op=ALU.mult)
                        v.tensor_tensor(out=t_sb[:], in0=sg_sb[:], in1=sg_sb[:], op=ALU.mult)
                        v.drain()
                        v.tensor_tensor(out=t_sb[:], in0=sg_sb[:], in1=t_sb[:],
                                        op=ALU.subtract)
                        v.drain()
                        v.tensor_tensor(out=u_sb[:], in0=pA[:], in1=t_sb[:], op=ALU.mult)
                        v.drain().then_inc(sV.h, 1)
                        v.tensor_tensor(out=t_sb[:], in0=sg_sb[:], in1=u_sb[:], op=ALU.add)
                        v.tensor_tensor(out=hid_bf[:], in0=h_sb[:], in1=pB[:], op=ALU.mult)
                        v.drain()
                        v.tensor_tensor(out=u_sb[:], in0=pB[:], in1=t_sb[:], op=ALU.mult)
                        v.drain().then_inc(sV.h, 1)
                    sV.n += 2
                    tok_gatefree = sV.n - 1

                    if p1stage < 2:
                        continue
                    @block.tensor
                    def _(te, h=h, t=tok_gatefree):
                        te.wait_ge(sV.h, t)
                        for a in range(A):
                            for j in range(2):
                                _mi = te.matmul(pA[:, a * 256:(a + 1) * 256], cfm(j, a),
                                          lw1[:, (h * 2 + j) * 256:(h * 2 + j + 1) * 256],
                                          start=(a % 2 == 0 and j == 0), stop=(j == 1))
                        _mi.then_inc(sPE.h, 1)
                    sPE.n += 1
                    tok_P = sPE.n

                    @block.vector
                    def _(v, t=tok_P):
                        v.wait_ge(sPE.h, t)
                        v.tensor_tensor(out=pgu_bf[:], in0=pA[:], in1=u_sb[:], op=ALU.mult)
                        v.tensor_tensor(out=ph_bf[:], in0=pA[:], in1=h_sb[:], op=ALU.mult)
                        v.drain().then_inc(sV.h, 1)
                    sV.n += 1
                    tok_sA_done = sV.n

                    if p1stage < 3:
                        continue
                    @block.tensor
                    def _(te, h=h, t=tok_sA_done):
                        te.wait_ge(sV.h, t)
                        for m in range(2):
                            _mi = te.matmul(pB[:, m * 256:m * 256 + 256], identf[:],
                                      lw1T[:, h * 512 + m * 256:h * 512 + (m + 1) * 256],
                                      start=(m == 0), stop=False)
                            for a in range(A):
                                _mi = te.matmul(pB[:, m * 256:m * 256 + 256],
                                          hid_bf[:, a * 256 + m * 128:a * 256 + m * 128 + 128],
                                          ctm(a),
                                          start=False, stop=(m == 1 and a == A - 1))
                        for j in range(2):
                            _mi = te.matmul(pB[:, 512 + j * 256:512 + j * 256 + 256], identf[:],
                                      lw0T[:, h * 512 + j * 256:h * 512 + (j + 1) * 256],
                                      start=(j == 0), stop=False)
                            for a in range(A):
                                _mi = te.matmul(pB[:, 512 + j * 256:512 + j * 256 + 256],
                                          ctm(a, j), pgu_bf[:, a * 256:(a + 1) * 256],
                                          start=False, stop=(j == 1 and a == A - 1))
                        for j in range(2):
                            _mi = te.matmul(pB[:, 1024 + j * 256:1024 + j * 256 + 256], identf[:],
                                      lw2T[:, h * 512 + j * 256:h * 512 + (j + 1) * 256],
                                      start=(j == 0), stop=False)
                            for a in range(A):
                                _mi = te.matmul(pB[:, 1024 + j * 256:1024 + j * 256 + 256],
                                          ctm(a, j), ph_bf[:, a * 256:(a + 1) * 256],
                                          start=False, stop=(j == 1 and a == A - 1))
                        _mi.then_inc(sPE.h, 1)
                    sPE.n += 1
                    tok_B = sPE.n
                    tok_ctm_pe = tok_B

                    @block.vector
                    def _(v, t=tok_B):
                        v.wait_ge(sPE.h, t)
                        v.tensor_copy(out=wn1[:], in_=pB[:, 0:512])
                        v.tensor_copy(out=wn0[:], in_=pB[:, 512:1024])
                        v.tensor_copy(out=wn2[:], in_=pB[:, 1024:1536])
                        v.drain().then_inc(sV.h, 1)
                    sV.n += 1
                    tok_wn = sV.n

                    for t2 in range(2 if p1stage >= 4 else 0):
                        @block.tensor
                        def _(te, h=h, t2=t2, t=tok_wn):
                            te.wait_ge(sV.h, t)
                            for me in range(2):
                                for j in range(2):
                                    _mi = te.matmul(pA[:, me * 512:(me + 1) * 512],
                                              wn0[:, j * 256 + me * 128:j * 256 + me * 128 + 128],
                                              cfmh(j, t2), start=(j == 0), stop=(j == 1))
                                for j in range(2):
                                    _mi = te.matmul(pA[:, 1024 + me * 512:1024 + (me + 1) * 512],
                                              wn2[:, j * 256 + me * 128:j * 256 + me * 128 + 128],
                                              cfmh(j, t2), start=(j == 0), stop=(j == 1))
                            _mi.then_inc(sPE.h, 1)
                        sPE.n += 1
                        tok_gq = sPE.n

                        @block.scalar
                        def _(sc, t=tok_gq):
                            sc.wait_ge(sPE.h, t)
                            sc.activation(out=sg_sb[:, 0:1024], in_=pA[:, 0:1024],
                                          func=AF.Sigmoid).then_inc(sS.h, 1)
                        sS.n += 1
                        tok_sqg = sS.n

                        @block.vector
                        def _(v, t=tok_sqg):
                            v.wait_ge(sS.h, t)
                            v.tensor_tensor(out=sg_sb[:, 0:1024], in0=pA[:, 0:1024],
                                            in1=sg_sb[:, 0:1024], op=ALU.mult)
                            v.drain()
                            v.tensor_tensor(out=hid_bf[:, 0:1024], in0=sg_sb[:, 0:1024],
                                            in1=pA[:, 1024:2048], op=ALU.mult)
                            v.drain().then_inc(sV.h, 1)
                        sV.n += 1
                        tok_hq2 = sV.n

                        @block.tensor
                        def _(te, h=h, t2=t2, t=tok_hq2):
                            te.wait_ge(sV.h, t)
                            for ds in range(2):
                                for me in range(2):
                                    _mi = te.matmul(pB[:, ds * 512:(ds + 1) * 512],
                                              wn1[:, me * 256 + ds * 128:me * 256 + ds * 128 + 128],
                                              hid_bf[:, me * 512:(me + 1) * 512],
                                              start=(me == 0), stop=(me == 1))
                            _mi.then_inc(sPE.h, 1)
                        sPE.n += 1
                        tok_oq = sPE.n

                        @block.vector
                        def _(v, t=tok_oq, w=tok_scrl):
                            v.wait_ge(sPE.h, t)
                            if w:
                                v.wait_ge(sG.h, w)
                            v.tensor_copy(out=oq_sb[:], in_=pB[:, 0:1024])
                            v.drain().then_inc(sV.h, 1)
                        sV.n += 1
                        tok_oqsb = sV.n

                        @block.gpsimd
                        def _(gp, h=h, t2=t2, n=n, t=tok_oqsb):
                            gp.wait_ge(sV.h, t)
                            for ds in range(2):
                                gp.dma_start(
                                    out=d_scrl[h * 256 + ds * 128:h * 256 + ds * 128 + 128,
                                               n * CH + t2 * 512:n * CH + t2 * 512 + 512],
                                    in_=oq_sb[:, ds * 512:(ds + 1) * 512],
                                ).then_inc(sG.h, 16)
                        sG.n += 32
                        tok_scrl = sG.n
                        tok_oqprev = tok_oqsb

                tok_xn2_pe = sPE.n

            tok_p1_pe = sPE.n
            tok_p1_v = sV.n

        tap("stats", stats_a[:], [128, 64])
        tap("xn2fm", d_xn2fm[:], [D, S], BF16)
        tap("scrl", d_scrl[:], [D, S])

        if nphase < 2:
            finish()
            return nc

        s0 = contextlib.ExitStack()
        with s0:
            e0 = s0.enter_context
            gbuf = e0(nc.sbuf_tensor("gbuf", [128, 2 * DFF], F32))
            gsum = e0(nc.sbuf_tensor("gsum", [128, 16], F32))
            gtot = e0(nc.sbuf_tensor("gtot", [128, 2], F32))
            gflat = e0(nc.sbuf_tensor("gflat", [1, 256], F32))
            gg = e0(nc.sbuf_tensor("gg", [128, 2], F32))

            g_red = {}
            for i in range(16):
                src = (d_fw1T.rearrange("(c p) f -> c p f", p=128)[i] if i < 8
                       else d_fw2T.rearrange("(g c p) f -> g p c f", p=128, c=4)[i - 8])
                buf = gbuf[:, (i % 2) * DFF:(i % 2 + 1) * DFF]
                bufdma = (buf.rearrange("p (c f) -> p c f", c=4) if i >= 8 else buf)

                @block.scalar
                def _(sc, src=src, bufdma=bufdma, i=i, tv=tok_p1_v, tg=tok_scrl):
                    if i >= 2:
                        sc.wait_ge(sV.h, g_red[i - 2])
                    else:
                        sc.wait_ge(sV.h, tv)
                        sc.wait_ge(sG.h, tg)
                    sc.dma_start(out=bufdma, in_=src).then_inc(sA.h, 16)
                sA.n += 16
                tok_gd = sA.n

                @block.vector
                def _(v, buf=buf, i=i, t=tok_gd):
                    v.wait_ge(sA.h, t)
                    v.reduce_sum(out=gsum[:, i:i + 1], in_=buf, axis=AX.X,
                                 apply_absolute_value=True).then_inc(sV.h, 1)
                sV.n += 1
                g_red[i] = sV.n

            @block.vector
            def _(v):
                v.drain()
                v.reduce_sum(out=gtot[:, 0:1], in_=gsum[:, 0:8], axis=AX.X)
                v.reduce_sum(out=gtot[:, 1:2], in_=gsum[:, 8:16], axis=AX.X)
                v.drain().then_inc(sV.h, 1)
            sV.n += 1
            tok_gt = sV.n

            @block.sync
            def _(sync, t=tok_gt):
                sync.wait_ge(sV.h, t)
                sync.dma_start(out=d_gg, in_=gtot[:]).then_inc(sQ.h, 16)
                sync.wait_ge(sQ.h, sQ.n + 16)
                sync.dma_start(out=gflat[:],
                               in_=d_gg.rearrange("p c -> (p c)").unsqueeze(0)
                               ).then_inc(sQ.h, 16)
            sQ.n += 32
            tok_gf = sQ.n

            @block.vector
            def _(v, t=tok_gf):
                v.wait_ge(sQ.h, t)
                v.memset(gg[:], 0.0)
                v.drain()
                v.reduce_sum(out=gg[0:1, 0:1], in_=gflat[:, 0:256:2], axis=AX.X)
                v.reduce_sum(out=gg[0:1, 1:2], in_=gflat[:, 1:256:2], axis=AX.X)
                v.drain()
                v.tensor_scalar(out=gg[0:1, :], in0=gg[0:1, :], scalar1=1.0 / (DFF * D),
                                scalar2=EPS, op0=ALU.mult, op1=ALU.add)
                v.drain()
                v.reciprocal(out=gg[0:1, :], in_=gg[0:1, :])
                v.drain().then_inc(sV.h, 1)
            sV.n += 1
            tok_ggv = sV.n

            @block.tensor
            def _(te, t=tok_ggv, tp=tok_p1_v):
                te.wait_ge(sV.h, max(t, tp))
                _mi = te.matmul(pF[:, 0:2], O_f[:], gg[:], start=True, stop=True)
                _mi.then_inc(sPE.h, 1)
            sPE.n += 1
            tok_rmm = sPE.n

            @block.vector
            def _(v, t=tok_rmm):
                v.wait_ge(sPE.h, t)
                v.tensor_copy(out=r_bc[:], in_=pF[:, 0:2])
                v.drain().then_inc(sV.h, 1)
            sV.n += 1
            tok_p0_v = sV.n

        tap("rbc", r_bc[:], [128, 2])
        tap("scrao", d_scrao[:], [D, S], BF16)
        tap("dbgm", d_dbgm[:], [16, S], BF16)
        tap("dbgqk", d_dbgqk[:], [128, S], BF16)
        tap("dbgq2", d_dbgq2[:], [128, S], BF16)
        tap("dbgw", d_dbgw[:], [128, 3072], BF16)
        tap("scra", d_scra[:], [D, S])

        if nphase < 3:
            finish()
            return nc

        s2 = contextlib.ExitStack()
        with s2:
            e2 = s2.enter_context
            XN2FM = e2(nc.sbuf_tensor("XN2FM", [128, 8 * S], BF16))
            wq_st = e2(nc.sbuf_tensor("wq_st", [128, 3072], F32))
            wq_bf = e2(nc.sbuf_tensor("wq_bf", [128, 3072], BF16))
            Qpack = e2(nc.sbuf_tensor("Qpack", [128, 4096], BF16))
            Kpack = e2(nc.sbuf_tensor("Kpack", [128, 4096], BF16))
            QaugA = e2(nc.sbuf_tensor("QaugA", [65, 4096], BF16))
            QaugB = e2(nc.sbuf_tensor("QaugB", [65, 4096], BF16))
            KaugA = e2(nc.sbuf_tensor("KaugA", [65, 4096], BF16))
            KaugB = e2(nc.sbuf_tensor("KaugB", [65, 4096], BF16))
            Vaug = e2(nc.sbuf_tensor("Vaug", [128, 32 * 160], BF16))
            m0n = e2(nc.sbuf_tensor("m0n", [128, 64], BF16))
            m0t_sb = e2(nc.sbuf_tensor("m0t_sb", [32, 256], BF16))
            mtmp = e2(nc.sbuf_tensor("mtmp", [128, 16], F32))
            prbuf = e2(nc.sbuf_tensor("prbuf", [128, 2 * 8192], BF16))
            ao_sb = e2(nc.sbuf_tensor("ao_sb", [65, 1024], F32))
            rr_pad = e2(nc.sbuf_tensor("rr_pad", [128, 1024], F32))
            bvb = e2(nc.sbuf_tensor("bvb", [128, 128], F32))
            rb_bc = e2(nc.sbuf_tensor("rb_bc", [128, 1024], F32))
            aostg = e2(nc.sbuf_tensor("aostg", [64, 4096], BF16))

            @block.vector
            def _(v):
                v.memset(rr_pad[:], 0.0)
                v.memset(Vaug[:], 1.0)
                v.memset(KaugA[64:65, :], 1.0)
                v.memset(KaugB[64:65, :], 1.0)
                v.drain().then_inc(sV.h, 1)
            sV.n += 1
            tok_p2pre = sV.n

            @block.sync
            def _(sync, t=tok_p0_v):
                sync.wait_ge(sV.h, t)
                for fc in range(8):
                    sync.dma_start(
                        out=XN2FM[:, fc * S:(fc + 1) * S],
                        in_=d_xn2fm.rearrange("(fc p) t -> fc p t", p=128)[fc],
                    ).then_inc(sQ.h, 16)
            sQ.n += 8 * 16
            tok_xn2ld = sQ.n

            tok_v_ev = 0
            tok_scrao = 0
            tok_aug_pe = 0
            tok_exp_last = 0
            tok_aoev_last = 0
            sVv_tok = {}

            for og in range(p2og):
                @block.gpsimd
                def _(gp, og=og, w=(tok_v_ev or tok_p0_v)):
                    gp.wait_ge(sV.h, w)
                    gp.dma_start(
                        out=bvb[:],
                        in_=d_brow[0:1, 2048 + og * 128:2048 + og * 128 + 128]
                            .partition_broadcast(128)).then_inc(sG.h, 16)
                    for sl in range(3):
                        gp.dma_start(
                            out=wq_st[:, sl * 1024:(sl + 1) * 1024],
                            in_=d_wqkvT.rearrange("(ic p) o -> p ic o", p=128)[
                                :, :, sl * 1024 + og * 128:sl * 1024 + og * 128 + 128],
                        ).then_inc(sG.h, 16)
                sG.n += 64
                tok_wq = sG.n

                @block.vector
                def _(v, t=tok_wq):
                    v.wait_ge(sG.h, t)
                    v.tensor_copy(out=wq_bf[:], in_=wq_st[:]).then_inc(sV.h, 1)
                sV.n += 1
                tok_wqb = sV.n

                @block.tensor
                def _(te, og=og, t=tok_wqb, tpre=tok_p2pre,
                      tse=tok_exp_last, tae=tok_aoev_last, tx=tok_xn2ld):
                    te.wait_ge(sV.h, max(t, tpre))
                    if og == 0:
                        te.wait_ge(sQ.h, tx)
                    if tse:
                        te.wait_ge(sS.h, tse)
                    if tae:
                        te.wait_ge(sV.h, tae)
                    for t8 in range(8):
                        for ic in range(8):
                            _mi = te.matmul(pF[:, t8 * 512:(t8 + 1) * 512],
                                      wq_bf[:, ic * 128:ic * 128 + 128],
                                      XN2FM[:, ic * S + t8 * 512:ic * S + t8 * 512 + 512],
                                      start=(ic == 0), stop=(ic == 7))
                    _mi.then_inc(sPE.h, 1)
                sPE.n += 1
                tok_qmm = sPE.n

                @block.scalar
                def _(sc, og=og, t=tok_qmm):
                    sc.wait_ge(sPE.h, t)
                    sc.activation(
                        out=Qpack[:], in_=pF[:], func=AF.Identity,
                        bias=bq2[:, og:og + 1], scale=0.125,
                    ).then_inc(sS.h, 1)
                sS.n += 1
                tok_qev = sS.n

                @block.tensor
                def _(te, og=og, t=tok_qev):
                    te.wait_ge(sS.h, t)
                    for t8 in range(8):
                        for ic in range(8):
                            _mi = te.matmul(pF[:, t8 * 512:(t8 + 1) * 512],
                                      wq_bf[:, 1024 + ic * 128:1024 + ic * 128 + 128],
                                      XN2FM[:, ic * S + t8 * 512:ic * S + t8 * 512 + 512],
                                      start=(ic == 0), stop=(ic == 7))
                    _mi.then_inc(sPE.h, 1)
                sPE.n += 1
                tok_kmm = sPE.n

                @block.scalar
                def _(sc, og=og, t=tok_kmm):
                    sc.wait_ge(sPE.h, t)
                    sc.activation(out=Kpack[:], in_=pF[:], func=AF.Identity,
                                  bias=bcols[:, 8 + og:9 + og]).then_inc(sS.h, 1)
                sS.n += 1
                tok_kev = sS.n

                @block.vector
                def _(v, t=tok_kev, ta=tok_aug_pe):
                    v.wait_ge(sS.h, t)
                    if ta:
                        v.wait_ge(sPE.h, ta)
                    v.tensor_copy(out=QaugA[0:64, :], in_=Qpack[0:64, :])
                    v.tensor_copy(out=KaugA[0:64, :], in_=Kpack[0:64, :])
                    v.drain().then_inc(sV.h, 1)
                sV.n += 1
                tok_augA = sV.n

                @block.sync
                def _(sync, t=tok_augA):
                    sync.wait_ge(sV.h, t)
                    sync.dma_start(out=QaugB[0:64, :], in_=Qpack[64:128, :]).then_inc(sQ.h, 16)
                    sync.dma_start(out=KaugB[0:64, :], in_=Kpack[64:128, :]).then_inc(sQ.h, 16)
                sQ.n += 32
                tok_augB = sQ.n

                if p2stage < 2:
                    continue
                for g4 in range(8):
                    @block.tensor
                    def _(te, og=og, g4=g4, t=tok_kev):
                        te.wait_ge(sS.h, t)
                        if g4 >= 2:
                            te.wait_ge(sV.h, sVv_tok[(og, g4 - 2)])
                        bank = pF[:, (g4 % 2) * 512:(g4 % 2) * 512 + 512]
                        for tt in range(4):
                            a_ = g4 * 4 + tt
                            for ic in range(8):
                                _mi = te.matmul(bank[:, tt * 128:(tt + 1) * 128],
                                          XN2FM[:, ic * S + a_ * 128:ic * S + a_ * 128 + 128],
                                          wq_bf[:, 2048 + ic * 128:2048 + ic * 128 + 128],
                                          start=(tt == 0 and ic == 0),
                                          stop=(ic == 7))
                        _mi.then_inc(sPE.h, 1)
                    sPE.n += 1
                    tok_vg = sPE.n

                    @block.vector
                    def _(v, og=og, g4=g4, t=tok_vg):
                        v.wait_ge(sPE.h, t)
                        for tt in range(4):
                            v.tensor_tensor(
                                out=Vaug[:].rearrange("p (t hh q) -> p t hh q",
                                                      hh=2, q=80)[:, g4 * 4 + tt, :, 0:64],
                                in0=pF[:, (g4 % 2) * 512 + tt * 128:
                                       (g4 % 2) * 512 + (tt + 1) * 128].rearrange(
                                    "p (hh d) -> p hh d", hh=2),
                                in1=bvb[:].rearrange("p (hh d) -> p hh d", hh=2),
                                op=ALU.add)
                        v.drain().then_inc(sV.h, 1)
                    sV.n += 1
                    sVv_tok[(og, g4)] = sV.n
                tok_v_ev = sV.n

                if p2stage < 3 or (30 <= p2stage < 33 and False):
                    pass
                if p2stage < 3:
                    continue
                p1banks = [pF[:, 2048 + i * 512:2048 + (i + 1) * 512] for i in range(4)]
                tok_red = {}
                for hh in ([0, 1] if p2w < 100 else ([0] if p2w < 200 else [1])):
                    Qsrc = QaugA if hh == 0 else QaugB
                    Ksrc = KaugA if hh == 0 else KaugB
                    for w in range(p2w % 100):
                        for kq in range(8):
                            li = w * 8 + kq
                            gli = hh * 256 + li

                            @block.tensor
                            def _(te, og=og, hh=hh, w=w, kq=kq, li=li, gli=gli,
                                  Qsrc=Qsrc, Ksrc=Ksrc, taug=tok_augB, tv=tok_v_ev):
                                if gli == 0:
                                    te.wait_ge(sQ.h, taug)
                                    te.wait_ge(sV.h, tv)
                                if gli >= 4:
                                    te.wait_ge(sV.h, tok_red[gli - 4])
                                _mi = te.matmul(p1banks[li % 4],
                                          Qsrc[0:64, w * 128:(w + 1) * 128],
                                          Ksrc[0:64, kq * 512:(kq + 1) * 512],
                                          start=True, stop=True)
                                _mi.then_inc(sPE.h, 1)
                            sPE.n += 1
                            tok_p1mm = sPE.n

                            @block.vector
                            def _(v, li=li, kq=kq, t=tok_p1mm):
                                v.wait_ge(sPE.h, t)
                                v.reduce_max(out=mtmp[:, kq:kq + 1],
                                             in_=p1banks[li % 4], axis=AX.X,
                                             negate=True).then_inc(sV.h, 1)
                            sV.n += 1
                            tok_red[gli] = sV.n

                        @block.vector
                        def _(v, hh=hh, w=w):
                            v.drain()
                            if p2stage != 30:
                                v.tensor_reduce(out=m0n[:, hh * 32 + w:hh * 32 + w + 1],
                                                in_=mtmp[:, 0:8], axis=AX.X, op=ALU.min)
                            v.drain().then_inc(sV.h, 1)
                        sV.n += 1
                        tok_red[1000 + w] = sV.n
                tok_p1v = sV.n

                if p2stage in (30, 31, 32):
                    continue
                @block.tensor
                def _(te, t=tok_p1v):
                    te.wait_ge(sV.h, t)
                    te.matmul(pF[0:32, 2048:2176], m0n[:, 0:32], ident[:],
                              start=True, stop=True)
                    _mi = te.matmul(pF[0:32, 2176:2304], m0n[:, 32:64], ident[:],
                                    start=True, stop=True)
                    _mi.then_inc(sPE.h, 1)
                sPE.n += 1
                tok_m0t = sPE.n

                @block.vector
                def _(v, t=tok_m0t):
                    v.wait_ge(sPE.h, t)
                    v.tensor_copy(out=m0t_sb[:, 0:128], in_=pF[0:32, 2048:2176])
                    v.tensor_copy(out=m0t_sb[:, 128:256], in_=pF[0:32, 2176:2304])
                    v.drain().then_inc(sV.h, 1)
                sV.n += 1
                tok_m0v = sV.n

                @block.sync
                def _(sync, t=tok_m0v):
                    sync.wait_ge(sV.h, t)
                    sync.dma_start(out=d_m0, in_=m0t_sb[:, 0:128]).then_inc(sQ.h, 16)
                    sync.dma_start(out=d_m0b, in_=m0t_sb[:, 128:256]).then_inc(sQ.h, 16)
                    sync.wait_ge(sQ.h, sQ.n + 32)
                    sync.dma_start(out=QaugA[64:65, :],
                                   in_=d_m0.rearrange("t p -> (t p)").unsqueeze(0)
                                   ).then_inc(sQ.h, 16)
                    sync.dma_start(out=QaugB[64:65, :],
                                   in_=d_m0b.rearrange("t p -> (t p)").unsqueeze(0)
                                   ).then_inc(sQ.h, 16)
                sQ.n += 64
                tok_m0 = sQ.n

                if "dbgm" in dbg:
                  @block.sync
                  def _(sync, og=og, t=tok_m0):
                    sync.wait_ge(sQ.h, t)
                    sync.dma_start(out=d_dbgm[og * 2:og * 2 + 1, :],
                                   in_=QaugA[64:65, :]).then_inc(sQ.h, 16)
                    sync.dma_start(out=d_dbgm[og * 2 + 1:og * 2 + 2, :],
                                   in_=QaugB[64:65, :]).then_inc(sQ.h, 16)
                    if og == 0:
                        sync.dma_start(out=d_dbgqk[0:64, :],
                                       in_=QaugA[0:64, :]).then_inc(sQ.h, 16)
                        sync.dma_start(out=d_dbgqk[64:128, :],
                                       in_=KaugA[0:64, :]).then_inc(sQ.h, 16)
                        sync.dma_start(out=d_dbgq2[:],
                                       in_=Qpack[:]).then_inc(sQ.h, 16)
                        sync.dma_start(out=d_dbgw[:],
                                       in_=wq_bf[:]).then_inc(sQ.h, 16)
                  sQ.n += 32 + (64 if og == 0 else 0)

                if p2stage < 4 or p2stage in (30, 31, 32):
                    continue
                for hh in range(2):
                    Qa = QaugA if hh == 0 else QaugB
                    Ka = KaugA if hh == 0 else KaugB
                    vofs = 0 if hh == 0 else 80
                    sc_banks = [pF[:, 0:1024], pF[:, 1024:2048]]
                    aop = pF[0:65, 2048:3072]
                    tok_exp = {}
                    tok_aogrp = {}
                    tok_aoacc = {}
                    for jq in range(4):

                        def sc_burst(sg, jq=jq, hh=hh, Qa=Qa, Ka=Ka):
                            for k8 in range(8):
                                kc = sg * 8 + k8
                                li = jq * 32 + kc

                                @block.tensor
                                def _(te, kc=kc, li=li, sg=sg, Qa=Qa, Ka=Ka,
                                      tm0=tok_m0, tael=tok_aoev_last):
                                    if li == 0:
                                        te.wait_ge(sQ.h, tm0)
                                    if li % 32 in (0, 1) and tael:
                                        te.wait_ge(sV.h, tael)
                                    if li >= 2:
                                        te.wait_ge(sS.h, tok_exp[li - 2])
                                    sb = sc_banks[li % 2]
                                    for hf in range(2):
                                        _mi = te.matmul(sb[:, hf * 512:(hf + 1) * 512],
                                                  Ka[:, kc * 128:(kc + 1) * 128],
                                                  Qa[:, jq * 1024 + hf * 512:
                                                     jq * 1024 + (hf + 1) * 512],
                                                  start=True, stop=True)
                                    _mi.then_inc(sPE.h, 1)
                                sPE.n += 1
                                tok_scmm = sPE.n

                                @block.scalar
                                def _(sc, k8=k8, li=li, sg=sg, jq=jq, t=tok_scmm):
                                    sc.wait_ge(sPE.h, t)
                                    g = jq * 4 + sg
                                    if g >= 2:
                                        sc.wait_ge(sPE.h, tok_aogrp[g - 2])
                                    sc.activation(
                                        out=prbuf[:, (sg % 2) * 8192 + k8 * 1024:
                                                  (sg % 2) * 8192 + (k8 + 1) * 1024],
                                        in_=sc_banks[li % 2], func=AF.Exp
                                    ).then_inc(sS.h, 1)
                                sS.n += 1
                                tok_exp[li] = sS.n

                        def ao_group(sg, jq=jq, vofs=vofs):
                            g = jq * 4 + sg

                            @block.tensor
                            def _(te, sg=sg, g=g, jq=jq, vofs=vofs):
                                te.wait_ge(sS.h, tok_exp[jq * 32 + sg * 8 + 7])
                                if g >= 1:
                                    te.wait_ge(sV.h, tok_aoacc[g - 1])
                                for hf in range(2):
                                    for k8 in range(8):
                                        kc = sg * 8 + k8
                                        _mi = te.matmul(
                                            aop[:, hf * 512:(hf + 1) * 512],
                                            Vaug[:, kc * 160 + vofs:kc * 160 + vofs + 65],
                                            prbuf[:, (sg % 2) * 8192 + k8 * 1024 + hf * 512:
                                                  (sg % 2) * 8192 + k8 * 1024 + (hf + 1) * 512],
                                            start=(k8 == 0), stop=(k8 == 7))
                                _mi.then_inc(sPE.h, 1)
                            sPE.n += 1
                            tok_aogrp[g] = sPE.n

                            @block.vector
                            def _(v, sg=sg, t=tok_aogrp[g]):
                                v.wait_ge(sPE.h, t)
                                if sg == 0:
                                    v.tensor_copy(out=ao_sb[0:65, :], in_=aop[:])
                                else:
                                    v.tensor_tensor(out=ao_sb[0:65, :], in0=ao_sb[0:65, :],
                                                    in1=aop[:], op=ALU.add)
                                v.drain().then_inc(sV.h, 1)
                            sV.n += 1
                            tok_aoacc[g] = sV.n

                        sc_burst(0)
                        sc_burst(1)
                        ao_group(0)
                        sc_burst(2)
                        ao_group(1)
                        sc_burst(3)
                        ao_group(2)
                        ao_group(3)

                        @block.vector
                        def _(v, t=tok_aoacc[jq * 4 + 3]):
                            v.wait_ge(sV.h, t)
                            v.reciprocal(out=rr_pad[64:65, :], in_=ao_sb[64:65, :])
                            v.drain().then_inc(sV.h, 1)
                        sV.n += 1
                        tok_rr = sV.n

                        @block.tensor
                        def _(te, t=tok_rr, tse=tok_exp[jq * 32 + 31]):
                            te.wait_ge(sV.h, t)
                            te.wait_ge(sS.h, tse)
                            _mi = te.matmul(pF[:, 1024:1536], O64[:], rr_pad[:, 0:512],
                                      start=True, stop=True)
                            _mi = te.matmul(pF[:, 1536:2048], O64[:], rr_pad[:, 512:1024],
                                      start=True, stop=True)
                            _mi.then_inc(sPE.h, 1)
                        sPE.n += 1
                        tok_rb = sPE.n

                        @block.vector
                        def _(v, jq=jq, t=tok_rb, ts=tok_scrao):
                            v.wait_ge(sPE.h, t)
                            if ts:
                                v.wait_ge(sQ.h, ts)
                            v.tensor_copy(out=rb_bc[:], in_=pF[:, 1024:2048])
                            v.drain()
                            v.tensor_tensor(out=aostg[0:64, jq * 1024:(jq + 1) * 1024],
                                            in0=ao_sb[0:64, :],
                                            in1=rb_bc[0:64, :], op=ALU.mult)
                            v.drain().then_inc(sV.h, 1)
                        sV.n += 1
                        tok_aoev = sV.n
                        tok_aoev_last = tok_aoev

                    @block.sync
                    def _(sync, og=og, hh=hh, t=tok_aoev_last):
                        sync.wait_ge(sV.h, t)
                        sync.dma_start(
                            out=d_scrao[og * 128 + hh * 64:og * 128 + hh * 64 + 64, :],
                            in_=aostg[0:64, :],
                        ).then_inc(sQ.h, 16)
                    sQ.n += 16
                    tok_scrao = sQ.n
                    tok_exp_last = sS.n

                tok_aug_pe = sPE.n

            tok_p2_q = sQ.n

        if nphase < 4:
            finish()
            return nc

        s3 = contextlib.ExitStack()
        with s3:
            e3 = s3.enter_context
            woh_st = e3(nc.sbuf_tensor("woh_st", [64, 8 * D], F32))
            woh_bf = e3(nc.sbuf_tensor("woh_bf", [64, 16 * D], BF16))
            aoL = e3(nc.sbuf_tensor("aoL", [64, 2 * 16 * 512], BF16))
            p3stg = e3(nc.sbuf_tensor("p3stg", [128, 4096], F32))

            tok_wob = 0
            for wh in range(2):
                @block.sync
                def _(sync, wh=wh, t=tok_p2_q, tc=tok_wob):
                    if wh == 0:
                        sync.wait_ge(sQ.h, t)
                    else:
                        sync.wait_ge(sV.h, tc)
                    sync.dma_start(out=woh_st[:],
                                   in_=d_woTh[:, wh * 8 * D:(wh + 1) * 8 * D]
                                   ).then_inc(sQ.h, 16)
                sQ.n += 16
                tok_wo = sQ.n

                @block.vector
                def _(v, wh=wh, t=tok_wo):
                    v.wait_ge(sQ.h, t)
                    v.tensor_copy(out=woh_bf[:, wh * 8 * D:(wh + 1) * 8 * D],
                                  in_=woh_st[:]).then_inc(sV.h, 1)
                sV.n += 1
                tok_wob = sV.n

            tok_p3_ld = {}
            tok_p3_mm = {}
            tok_p3s_prev = 0
            tok_p3store_prev = 0
            for t8 in range(8):
                @block.gpsimd
                def _(gp, t8=t8, tq=tok_p2_q):
                    if t8 >= 2:
                        gp.wait_ge(sPE.h, tok_p3_mm[t8 - 2])
                    else:
                        gp.wait_ge(sQ.h, tq)
                    for hg in range(16):
                        gp.dma_start(
                            out=aoL[:, ((t8 % 2) * 16 + hg) * 512:
                                    ((t8 % 2) * 16 + hg + 1) * 512],
                            in_=d_scrao[hg * 64:(hg + 1) * 64,
                                        t8 * 512:(t8 + 1) * 512],
                        ).then_inc(sG.h, 16)
                sG.n += 16 * 16
                tok_p3_ld[t8] = sG.n

                @block.tensor
                def _(te, t8=t8, t=tok_p3_ld[t8], tw=tok_wob, tsp=tok_p3s_prev):
                    te.wait_ge(sG.h, t)
                    if t8 == 0:
                        te.wait_ge(sV.h, tw)
                    if tsp:
                        te.wait_ge(sS.h, tsp)
                    for do in range(8):
                        for hg in range(16):
                            _mi = te.matmul(
                                pF[:, do * 512:(do + 1) * 512],
                                woh_bf[:, hg * D + do * 128:hg * D + do * 128 + 128],
                                aoL[:, ((t8 % 2) * 16 + hg) * 512:
                                    ((t8 % 2) * 16 + hg + 1) * 512],
                                start=(hg == 0), stop=(hg == 15))
                    _mi.then_inc(sPE.h, 1)
                sPE.n += 1
                tok_p3_mm[t8] = sPE.n
                tok_p3mm = sPE.n

                @block.scalar
                def _(sc, t8=t8, t=tok_p3mm, tst=tok_p3store_prev):
                    sc.wait_ge(sPE.h, t)
                    if tst:
                        sc.wait_ge(sQ.h, tst)
                    for do in range(8):
                        ins = sc.activation(out=p3stg[:, do * 512:(do + 1) * 512],
                                            in_=pF[:, do * 512:(do + 1) * 512],
                                            func=AF.Identity, bias=wob[:, do:do + 1])
                    ins.then_inc(sS.h, 1)
                sS.n += 1
                tok_p3s_prev = sS.n

                @block.sync
                def _(sync, t8=t8, t=tok_p3s_prev):
                    sync.wait_ge(sS.h, t)
                    sync.dma_start(
                        out=d_scra[:, t8 * 512:(t8 + 1) * 512].rearrange(
                            "(do p) c -> p do c", p=128),
                        in_=p3stg[:].rearrange("p (do c) -> p do c", do=8),
                    ).then_inc(sQ.h, 16)
                sQ.n += 16
                tok_p3store_prev = sQ.n
            tok_p3_q = sQ.n


        if nphase < 5:
            finish()
            return nc

        s4 = contextlib.ExitStack()
        with s4:
            e4 = s4.enter_context
            Wq1 = e4(nc.sbuf_tensor("Wq1", [128, 8 * 4096], BF16))
            Wq2 = e4(nc.sbuf_tensor("Wq2", [128, 32 * 1024], BF16))
            stsb = e4(nc.sbuf_tensor("stsb", [8, 1024], F32))

            sqz = contextlib.ExitStack()
            with sqz:
                gstg = sqz.enter_context(nc.sbuf_tensor("gstg", [128, 2 * 4096], F32))
                qz_tok = {}
                for i in range(16):
                    if i < 8:
                        src = d_fw1T.rearrange("(c p) f -> c p f", p=128)[i]
                        dst = Wq1[:, i * 4096:(i + 1) * 4096]
                        rcol = r_bc[:, 0:1]
                        buf = gstg[:, (i % 2) * 4096:(i % 2 + 1) * 4096]
                        bufdma = buf
                    else:
                        src = d_fw2T.rearrange("(g c p) f -> g p c f", p=128, c=4)[i - 8]
                        dst = Wq2[:, (i - 8) * 4096:(i - 7) * 4096]
                        rcol = r_bc[:, 1:2]
                        buf = gstg[:, (i % 2) * 4096:(i % 2 + 1) * 4096]
                        bufdma = buf.rearrange("p (c f) -> p c f", c=4)

                    @block.scalar
                    def _(sc, src=src, buf=buf, i=i, tq=tok_p3_q):
                        if i == 0:
                            sc.wait_ge(sQ.h, tq)
                        if i >= 2:
                            sc.wait_ge(sV.h, qz_tok[i - 2])
                        sc.dma_start(out=bufdma, in_=src).then_inc(sA.h, 16)
                    sA.n += 16
                    tok_qd = sA.n

                    @block.vector
                    def _(v, buf=buf, dst=dst, rcol=rcol, i=i, t=tok_qd):
                        v.wait_ge(sA.h, t)
                        v.tensor_scalar(out=buf, in0=buf, scalar1=rcol,
                                        scalar2=RC, op0=ALU.mult, op1=ALU.add)
                        v.drain()
                        v.tensor_scalar(out=buf, in0=buf, scalar1=RC,
                                        scalar2=1.0, op0=ALU.subtract, op1=ALU.min)
                        v.drain()
                        v.tensor_scalar(out=dst, in0=buf, scalar1=-1.0,
                                        scalar2=None, op0=ALU.max)
                        v.drain().then_inc(sV.h, 1)
                    sV.n += 1
                    qz_tok[i] = sV.n
                tok_qz = sV.n
            mu_bc = e4(nc.sbuf_tensor("mu_bc", [128, CH], F32))
            rstd_bc = e4(nc.sbuf_tensor("rstd_bc", [128, CH], F32))
            xnf = e4(nc.sbuf_tensor("xnf", [128, 8 * 512], BF16))
            xtmp = e4(nc.sbuf_tensor("xtmp", [128, 1024], F32))
            h_all = e4(nc.sbuf_tensor("h_all", [128, 32 * 512], BF16))
            acc = e4(nc.sbuf_tensor("acc", [128, 8 * 512], F32))

            @block.tensor
            def _(te, ts3=tok_p3s_prev):
                if ts3:
                    te.wait_ge(sS.h, ts3)
                for i in range(8):
                    kind, n = i // 4, i % 4
                    _mi = te.matmul(pF[0:8, i * 128:(i + 1) * 128],
                                    stats_a[:, n * 16 + kind * 8:n * 16 + kind * 8 + 8],
                                    identf[:], start=(i % 4 == 0), stop=True)
                _mi.then_inc(sPE.h, 1)
            sPE.n += 1
            tok_stmm = sPE.n

            @block.vector
            def _(v, t=tok_stmm):
                v.wait_ge(sPE.h, t)
                v.tensor_copy(out=stsb[:], in_=pF[0:8, 0:1024])
                v.drain().then_inc(sV.h, 1)
            sV.n += 1
            tok_stsb = sV.n

            @block.sync
            def _(sync, t=tok_stsb):
                sync.wait_ge(sV.h, t)
                for i in range(8):
                    kind, n = i // 4, i % 4
                    sync.dma_start(
                        out=d_strows[kind:kind + 1, n * CH:(n + 1) * CH].rearrange(
                            "a (t p) -> a t p", p=128),
                        in_=stsb[:, i * 128:(i + 1) * 128],
                    ).then_inc(sQ.h, 16)
            sQ.n += 8 * 16
            tok_strows = sQ.n

            tok_acc_done = 0
            tok_ffn_v = 0
            tok_pbpe_prev = 0
            for n in range(NCH):
                @block.gpsimd
                def _(gp, n=n, t=tok_strows, tv=tok_ffn_v):
                    gp.wait_ge(sQ.h, t)
                    if tv:
                        gp.wait_ge(sV.h, tv)
                    gp.dma_start(
                        out=mu_bc[:],
                        in_=d_strows[0:1, n * CH:(n + 1) * CH].partition_broadcast(128),
                    ).then_inc(sG.h, 16)
                    gp.dma_start(
                        out=rstd_bc[:],
                        in_=d_strows[1:2, n * CH:(n + 1) * CH].partition_broadcast(128),
                    ).then_inc(sG.h, 16)
                sG.n += 32
                tok_bc = sG.n

                for half in range(2):
                    ts_ = n * 2 + half
                    xl_tok = {}
                    for fc in range(8):
                        @block.gpsimd
                        def _(gp, ts_=ts_, fc=fc, w=tok_ffn_v):
                            if fc == 0 and w:
                                gp.wait_ge(sV.h, w)
                            if fc >= 2:
                                gp.wait_ge(sV.h, xl_tok[fc - 2])
                            gp.dma_start(
                                out=xtmp[:, (fc % 2) * 512:(fc % 2 + 1) * 512],
                                in_=d_x_fm[fc * 128:(fc + 1) * 128,
                                           ts_ * 512:(ts_ + 1) * 512],
                            ).then_inc(sG.h, 16)
                        sG.n += 16
                        tok_xl = sG.n

                        @block.vector
                        def _(v, fc=fc, half=half, t=tok_xl, tb=tok_bc):
                            v.wait_ge(sG.h, max(t, tb))
                            src = xtmp[:, (fc % 2) * 512:(fc % 2 + 1) * 512]
                            msl = mu_bc[:, half * 512:(half + 1) * 512]
                            rsl = rstd_bc[:, half * 512:(half + 1) * 512]
                            v.tensor_tensor(out=src, in0=src, in1=msl, op=ALU.subtract)
                            v.drain()
                            v.tensor_tensor(out=xnf[:, fc * 512:(fc + 1) * 512],
                                            in0=src, in1=rsl, op=ALU.mult)
                            v.drain().then_inc(sV.h, 1)
                        sV.n += 1
                        xl_tok[fc] = sV.n
                    tok_xnf_all = sV.n

                    tok_hev = {}
                    for ffc in range(32):
                        @block.tensor
                        def _(te, ffc=ffc, t=tok_xnf_all, tq=tok_qz,
                              tpb=tok_pbpe_prev):
                            if ffc == 0:
                                te.wait_ge(sV.h, max(t, tq))
                            if ffc >= 2:
                                te.wait_ge(sS.h, tok_hev[ffc - 2])
                            bank = pF[:, (ffc % 2) * 512:(ffc % 2) * 512 + 512]
                            for ic in range(8):
                                _mi = te.matmul(bank,
                                          Wq1[:, ic * 4096 + ffc * 128:
                                              ic * 4096 + ffc * 128 + 128],
                                          xnf[:, ic * 512:(ic + 1) * 512],
                                          start=(ic == 0), stop=(ic == 7))
                            _mi.then_inc(sPE.h, 1)
                        sPE.n += 1
                        tok_hmm = sPE.n

                        @block.scalar
                        def _(sc, ffc=ffc, t=tok_hmm, tpb=tok_pbpe_prev):
                            sc.wait_ge(sPE.h, t)
                            if ffc == 0 and tpb:
                                sc.wait_ge(sPE.h, tpb)
                            sc.activation(out=h_all[:, ffc * 512:(ffc + 1) * 512],
                                          in_=pF[:, (ffc % 2) * 512:(ffc % 2) * 512 + 512],
                                          func=AF.Relu, bias=b1c[:, ffc:ffc + 1]
                                          ).then_inc(sS.h, 1)
                        sS.n += 1
                        tok_hev[ffc] = sS.n
                    tok_hall = sS.n

                    @block.gpsimd
                    def _(gp, ts_=ts_, w=tok_acc_done):
                        if w:
                            gp.wait_ge(sQ.h, w)
                        accv = acc[:].rearrange("p (fc c) -> p fc c", fc=8)
                        gp.dma_start(
                            out=accv,
                            in_=d_x_fm[:, ts_ * 512:(ts_ + 1) * 512].rearrange(
                                "(fc p) c -> p fc c", p=128),
                        ).then_inc(sG.h, 16)
                        gp.dma_start(
                            out=accv,
                            in_=d_scrl[:, ts_ * 512:(ts_ + 1) * 512].rearrange(
                                "(fc p) c -> p fc c", p=128),
                            accum_op=ALU.add,
                        ).then_inc(sG.h, 16)
                        gp.dma_start(
                            out=accv,
                            in_=d_scra[:, ts_ * 512:(ts_ + 1) * 512].rearrange(
                                "(fc p) c -> p fc c", p=128),
                            accum_op=ALU.add,
                        ).then_inc(sG.h, 16)
                    sG.n += 48
                    tok_acc = sG.n

                    tok_pb = {}
                    for dh in range(2):
                        @block.tensor
                        def _(te, dh=dh, t=tok_hall):
                            te.wait_ge(sS.h, t)
                            if dh == 1:
                                te.wait_ge(sV.h, tok_pb[0])
                            for do in range(4):
                                dog = dh * 4 + do
                                for ffc in range(32):
                                    _mi = te.matmul(pF[:, 2048 + do * 512:2048 + (do + 1) * 512],
                                              Wq2[:, ffc * 1024 + dog * 128:
                                                  ffc * 1024 + dog * 128 + 128],
                                              h_all[:, ffc * 512:(ffc + 1) * 512],
                                              start=(ffc == 0), stop=(ffc == 31))
                            _mi.then_inc(sPE.h, 1)
                        sPE.n += 1
                        tok_p4mm = sPE.n

                        @block.vector
                        def _(v, dh=dh, t=tok_p4mm, ta=tok_acc):
                            v.wait_ge(sPE.h, t)
                            v.wait_ge(sG.h, ta)
                            for do in range(4):
                                dog = dh * 4 + do
                                v.scalar_tensor_tensor(
                                    out=acc[:, dog * 512:(dog + 1) * 512],
                                    in0=pF[:, 2048 + do * 512:2048 + (do + 1) * 512],
                                    scalar=b2c[:, dog:dog + 1],
                                    in1=acc[:, dog * 512:(dog + 1) * 512],
                                    op0=ALU.add, op1=ALU.add)
                            v.drain().then_inc(sV.h, 1)
                        sV.n += 1
                        tok_pb[dh] = sV.n
                    tok_ffn_v = sV.n
                    tok_pbpe_prev = sPE.n

                    @block.sync
                    def _(sync, ts_=ts_, t=tok_ffn_v):
                        sync.wait_ge(sV.h, t)
                        sync.dma_start(
                            out=d_out[:, ts_ * 512:(ts_ + 1) * 512].rearrange(
                                "(fc p) c -> p fc c", p=128),
                            in_=acc[:].rearrange("p (fc c) -> p fc c", fc=8),
                        ).then_inc(sQ.h, 16)
                    sQ.n += 16
                    tok_acc_done = sQ.n

        finish()
    return nc


def host_inputs(inputs, core):
    x = np.ascontiguousarray(np.asarray(inputs["L0_attn_x"])[core])
    ipw = np.asarray(inputs["L0_attn_in_proj_w"])
    ipb = np.asarray(inputs["L0_attn_in_proj_b"])
    opw = np.asarray(inputs["L0_attn_out_proj_w"])
    opb = np.asarray(inputs["L0_attn_out_proj_b"])
    w0 = np.asarray(inputs["L0_attn_w0"])
    w1 = np.asarray(inputs["L0_attn_w1"])
    w2 = np.asarray(inputs["L0_attn_w2"])
    W1 = np.asarray(inputs["L0_ffn_W1"])
    b1 = np.asarray(inputs["L0_ffn_b1"])
    W2 = np.asarray(inputs["L0_ffn_W2"])
    b2 = np.asarray(inputs["L0_ffn_b2"])

    def lact_layout(w):
        return np.ascontiguousarray(
            w.reshape(NLH, 2, 128, 256).transpose(2, 0, 1, 3).reshape(128, 2048))

    woT = np.ascontiguousarray(opw.T)
    woTh = np.ascontiguousarray(
        woT.reshape(16, 64, D).transpose(1, 0, 2).reshape(64, 16 * D))

    return {
        "x_tm": x,
        "x_fm": np.ascontiguousarray(x.T),
        "wqkvT": np.ascontiguousarray(ipw.T),
        "woTh": woTh,
        "bcols": np.ascontiguousarray(ipb.reshape(24, 128).T),
        "brow": np.ascontiguousarray(ipb.reshape(1, 3 * D)),
        "wob": np.ascontiguousarray(opb.reshape(8, 128).T),
        "lw0T": lact_layout(w0.transpose(0, 2, 1)),
        "lw2T": lact_layout(w2.transpose(0, 2, 1)),
        "lw1": lact_layout(w1),
        "lw1T": lact_layout(w1.transpose(0, 2, 1)),
        "lnw": np.ascontiguousarray(np.asarray(inputs["L0_attn_ln_w"]).reshape(1, D)),
        "lnb": np.ascontiguousarray(np.asarray(inputs["L0_attn_ln_b"]).reshape(1, D)),
        "fw1T": np.ascontiguousarray(W1.T),
        "fw2T": np.ascontiguousarray(W2.T),
        "b1c": np.ascontiguousarray(b1.reshape(32, 128).T),
        "b2c": np.ascontiguousarray(b2.reshape(8, 128).T),
    }


def kernel(**inputs):
    nc = build()
    in_maps = [host_inputs(inputs, c) for c in range(8)]
    res = run_bass_kernel_spmd(nc, in_maps, core_ids=list(range(8)))
    outs = [np.ascontiguousarray(np.asarray(r["out_fm"]).T) for r in res.results]
    return np.stack(outs, axis=0)
